# version 03 (final)
# baseline (speedup 1.0000x reference)
import contextlib
import numpy as np
import concourse.bass as bass
import concourse.mybir as mybir
from concourse.bass_utils import run_bass_kernel_spmd

F32 = mybir.dt.float32
BF16 = mybir.dt.bfloat16
AF = mybir.ActivationFunctionType
ALU = mybir.AluOpType
AX = mybir.AxisListType

ENGS = ("sync", "tensor", "vector", "scalar", "gpsimd")

T = 4096
DM = 1024
DT = DM // 128



class SemCtr:
    def __init__(self, handle):
        self.h = handle
        self.n = 0

    def inc(self, k):
        self.n += k
        return self.n


class Prog:
    def __init__(self):
        self.nc = bass.Bass("TRN2", target_bir_lowering=False, debug=False)
        self.stack = contextlib.ExitStack()
        self.ops = {e: [] for e in ENGS}
        self._sems = {}
        self.pending_waits = {e: [] for e in ENGS}
        self._since_bar = {e: None for e in ENGS}
        self._dirty = set()

    def sem(self, name):
        if name not in self._sems:
            h = self.stack.enter_context(self.nc.semaphore(name))
            self._sems[name] = SemCtr(h)
        return self._sems[name]

    _uid = 0

    def sbuf(self, name, shape, dtype, stk=None):
        Prog._uid += 1
        name = f"{name}_{Prog._uid}"
        return (stk or self.stack).enter_context(self.nc.sbuf_tensor(name, shape, dtype))

    def psum(self, name, shape, dtype, stk=None):
        Prog._uid += 1
        name = f"{name}_{Prog._uid}"
        return (stk or self.stack).enter_context(self.nc.psum_tensor(name, shape, dtype))

    def op(self, eng, fn, waits=(), incs=(), is_dma=False):
        merged = {}
        for s, v in list(self.pending_waits[eng]) + [(s, v) for (s, v) in waits]:
            if v > merged.get(id(s), (s, -1))[1]:
                merged[id(s)] = (s, v)
        w = [sv for sv in merged.values() if sv[1] > 0]
        self.pending_waits[eng] = []
        for s, k in incs:
            s.inc(k)
            self._dirty.add(s)
        entry = [fn, tuple((s, v) for (s, v) in w), [(s, k) for (s, k) in incs], is_dma]
        self.ops[eng].append(entry)
        self._since_bar[eng] = len(self.ops[eng]) - 1
        return entry

    def dma(self, eng, out, in_, sem, **kw):
        return self.op(eng, lambda e, o=out, i=in_: e.dma_start(out=o, in_=i, **kw),
                       incs=[(sem, 16)], is_dma=True)

    def bar(self, tag=""):
        b = self.sem("bar")
        for e in ENGS:
            idx = self._since_bar[e]
            if idx is not None:
                entry = self.ops[e][idx]
                if not entry[2]:
                    k = 16 if entry[3] else 1
                    b.inc(k)
                    self._dirty.add(b)
                    entry[2].append((b, k))
            self._since_bar[e] = None
        tokens = [(s, s.n) for s in self._dirty]
        self._dirty = set()
        for e in ENGS:
            self.pending_waits[e].extend(tokens)
        return tokens

    def finalize(self):
        nc = self.nc
        with nc.Block() as blk:
            for eng in ENGS:
                def make(eng):
                    def run(e):
                        for fn, waits, incs, _ in self.ops[eng]:
                            for s, v in waits:
                                e.wait_ge(s.h, v)
                            inst = fn(e)
                            for s, k in incs:
                                inst.then_inc(s.h, k)
                    return run
                getattr(blk, eng)(make(eng))



def _ct(a):
    return np.ascontiguousarray(a, dtype=np.float32)


def _vec_tiles(v):
    v = np.asarray(v)
    n = v.shape[0] // 128
    return _ct(v.reshape(n, 128).T)


def _mslab(W, pad_m=0):
    W = np.asarray(W)
    O, I = W.shape
    if pad_m:
        W = np.pad(W, ((0, pad_m), (0, 0)))
        O += pad_m
    kp = 128 if I % 128 == 0 else I
    kt = I // kp
    A = W.T.reshape(kt, kp, O // 128, 128).transpose(2, 1, 0, 3)
    return _ct(A.reshape(O // 128, kp, kt * 128))


def prep_arrays(inputs):
    g = {}
    g["c_ident"] = _ct(np.eye(128))
    g["c_ones1"] = _ct(np.ones((1, 128)))
    g["c_onesK"] = _ct(np.ones((128, 1)))
    sel = np.zeros((16, 16 * 128), np.float32)
    for n in range(16):
        sel[n, n * 128:(n + 1) * 128] = 1.0
    g["c_sel16"] = sel
    gsel = np.zeros((32, 2), np.float32)
    gsel[0:16, 0] = 1.0 / 16
    gsel[16:32, 1] = 1.0 / 16
    g["c_gsel"] = gsel
    e2 = np.zeros((2, 256), np.float32)
    e2[0, 0:128] = 1.0
    e2[1, 128:256] = 1.0
    g["c_e2"] = e2

    for l in range(4):
        g[f"m{l}_wi"] = _mslab(inputs["L0_attn_in_proj_w"][l])
        cw = np.asarray(inputs["L0_attn_conv_w"][l])
        g[f"m{l}_cw"] = _ct(cw.reshape(16, 128, 4).transpose(1, 0, 2)
                            .reshape(128, 64))
        g[f"m{l}_cb"] = _vec_tiles(inputs["L0_attn_conv_b"][l])
        g[f"m{l}_wx"] = _mslab(inputs["L0_attn_x_proj_w"][l], pad_m=32)
        g[f"m{l}_wdt"] = _mslab(inputs["L0_attn_dt_proj_w"][l])
        g[f"m{l}_bdt"] = _vec_tiles(inputs["L0_attn_dt_proj_b"][l])
        al = np.asarray(inputs["L0_attn_A_log"][l])
        g[f"m{l}_Alog"] = _ct(al.reshape(16, 128, 16).transpose(1, 0, 2)
                              .reshape(128, 256))
        g[f"m{l}_D"] = _vec_tiles(inputs["L0_attn_D"][l])
        g[f"m{l}_wo"] = _mslab(inputs["L0_attn_out_proj_w"][l])

    g["f_w1"] = _mslab(inputs["L0_ffn_w1_w"])
    g["f_w2"] = _mslab(inputs["L0_ffn_w2_w"])
    g["f_h1"] = _mslab(inputs["L0_ffn_h1_w"])
    g["f_h2"] = _mslab(inputs["L0_ffn_h2_w"])
    g["f_w3"] = _mslab(inputs["L0_ffn_w3_w"])
    g["f_s1"] = _vec_tiles(inputs["L0_ffn_w1_s"])
    g["f_s2"] = _vec_tiles(inputs["L0_ffn_w2_s"])
    g["f_sh1"] = _vec_tiles(inputs["L0_ffn_h1_s"])
    g["f_sh2"] = _vec_tiles(inputs["L0_ffn_h2_s"])
    g["f_s3"] = _vec_tiles(inputs["L0_ffn_w3_s"])

    g["a_win"] = _mslab(inputs["L1_attn_W_in"])
    acw = np.asarray(inputs["L1_attn_conv_w"]).reshape(2048, 4)
    g["a_cw"] = _ct(acw.reshape(16, 128, 4).transpose(1, 0, 2).reshape(128, 64))
    g["a_cb"] = _vec_tiles(inputs["L1_attn_conv_b"])
    wxr = np.asarray(inputs["L1_attn_W_x"])
    wxp = np.concatenate([wxr[32:], wxr[0:16], wxr[16:32]], 0)
    g["a_wx"] = _mslab(wxp, pad_m=96)
    g["a_wdt"] = _mslab(inputs["L1_attn_W_dt"])
    g["a_bdt"] = _vec_tiles(inputs["L1_attn_b_dt"])
    g["a_D"] = _vec_tiles(inputs["L1_attn_D"])
    g["a_wo"] = _mslab(inputs["L1_attn_W_out"])

    g["g_ln1"] = _vec_tiles(inputs["L1_ffn_ln1_w"])
    g["g_ln2"] = _vec_tiles(inputs["L1_ffn_ln2_w"])
    g["g_kvd"] = _mslab(inputs["L1_ffn_kv_d"])
    g["g_qd"] = _mslab(inputs["L1_ffn_q_d"])
    g["g_ku"] = _mslab(inputs["L1_ffn_k_u"])
    g["g_qu"] = _mslab(inputs["L1_ffn_q_u"])
    g["g_vu"] = _mslab(inputs["L1_ffn_v_u"])
    g["g_rq"] = _mslab(inputs["L1_ffn_rope_q_w"])
    g["g_rk"] = _mslab(inputs["L1_ffn_rope_k_w"])
    g["g_ow"] = _mslab(inputs["L1_ffn_o_w"])
    g["g_routerT"] = _ct(inputs["L1_ffn_router_w"].T)
    g["g_rbias"] = _ct(np.asarray(inputs["L1_ffn_routing_bias"]).reshape(1, 7))
    g["g_sg"] = _mslab(inputs["L1_ffn_shared_gate"][0])
    g["g_su"] = _mslab(inputs["L1_ffn_shared_up"][0])
    g["g_sd"] = _mslab(inputs["L1_ffn_shared_down"][0])
    for e in range(7):
        g[f"g_rg{e}"] = _mslab(inputs["L1_ffn_routed_gate"][e])
        g[f"g_ru{e}"] = _mslab(inputs["L1_ffn_routed_up"][e])
        g[f"g_rd{e}"] = _mslab(inputs["L1_ffn_routed_down"][e])

    RD = 32
    inv_freq = 1.0 / (10000.0 ** (np.arange(0, RD, 2, dtype=np.float64) / RD))
    tpos = np.arange(T, dtype=np.float64)
    freqs = tpos[:, None] * inv_freq[None, :]
    emb = np.concatenate([freqs, freqs], axis=-1)
    cos = np.cos(emb)
    sin = np.sin(emb)
    cos_fm = np.tile(cos.T, (4, 1))
    sin_t = sin.T
    sgn = np.ones((32, 1)); sgn[:16] = -1.0
    sgnsin_fm = np.tile(sin_t * sgn, (4, 1))
    g["c_cos"] = _ct(cos_fm)
    g["c_sgnsin"] = _ct(sgnsin_fm)

    masks = np.zeros((4, 128, 512), np.float32)
    for p in range(4):
        qi = p * 128 + np.arange(128)[:, None]
        kj = np.arange(512)[None, :]
        masks[p] = np.where(kj <= qi, 0.0, -1e30)
    g["c_masks"] = _ct(masks.transpose(1, 0, 2).reshape(128, 4 * 512))

    return g


def shard_inputs(inputs):
    g = prep_arrays(inputs)
    x = np.asarray(inputs["L0_attn_x"])
    maps = []
    for c in range(8):
        m = dict(g)
        m["x0T"] = _ct(x[c].T)
        maps.append(m)
    return maps



class KB:

    def __init__(self):
        self.p = Prog()
        self.nc = self.p.nc
        self.d = {}

    def declare(self, maps0):
        nc = self.nc
        for name, arr in maps0.items():
            self.d[name] = nc.dram_tensor(name, list(arr.shape), F32,
                                          kind="ExternalInput").ap()
        self.d["out"] = nc.dram_tensor("out", [DM, T], F32,
                                       kind="ExternalOutput").ap()
        def scratch(name, shape, dt):
            self.d[name] = nc.dram_tensor(name, shape, dt, kind="Internal").ap()
        scratch("sc_ra", [DM, T], F32)
        scratch("sc_rb", [DM, T], F32)
        scratch("sc_bo", [2048, T], F32)
        scratch("sc_xa", [2048, T], BF16)
        scratch("sc_xb", [2048, T], BF16)
        scratch("sc_t1", [DM, T], F32)
        scratch("sc_xg", [DM, T], F32)
        scratch("sc_xz", [DM, T], F32)
        scratch("sc_qk", [4096, T], BF16)

    def load_consts(self):
        p, d = self.p, self.d
        self.ident = p.sbuf("ident", [128, 128], BF16)
        self.ones1 = p.sbuf("ones1", [1, 128], F32)
        self.onesK = p.sbuf("onesK", [128, 1], F32)
        ld = p.sem("ld0")
        ldg = p.sem("ld0g")
        p.dma("gpsimd", self.ident.ap(), d["c_ident"], ldg)
        p.dma("sync", self.ones1.ap(), d["c_ones1"], ld)
        p.dma("sync", self.onesK.ap(), d["c_onesK"], ld)
        p.bar("consts")

    def ln(self, src, dst_f32=None, dst_bf=None, add=None, gamma=None, tag="ln"):
        p, nc = self.p, self.nc
        TCL = 512
        NCH = T // TCL
        with contextlib.ExitStack() as stk:
            xt = [p.sbuf(f"{tag}_x{i}", [128, TCL], F32, stk) for i in range(DT)]
            at = [p.sbuf(f"{tag}_a{i}", [128, TCL], F32, stk) for i in range(DT)] \
                if add is not None else None
            x2 = [p.sbuf(f"{tag}_sq{i}", [128, TCL], F32, stk) for i in range(DT)]
            rows = p.sbuf(f"{tag}_rows", [1, 6 * TCL], F32, stk)
            o1 = [p.sbuf(f"{tag}_o{i}", [128, TCL], F32, stk) for i in range(DT)]
            ps_s = p.psum(f"{tag}_ps", [128, 1024], F32, stk)
            ps_b = p.psum(f"{tag}_pb", [128, 1024], F32, stk)
            ld = p.sem("ldx")
            st = p.sem("stoA")
            pe = p.sem("lpe")
            ve = p.sem("xve")
            se = p.sem("xse")
            for c in range(NCH):
                cs = slice(c * TCL, (c + 1) * TCL)
                for i in range(DT):
                    p.dma("sync", xt[i].ap(), src[i * 128:(i + 1) * 128, cs], ld)
                if add is not None:
                    for i in range(DT):
                        p.dma("sync", at[i].ap(), add[i * 128:(i + 1) * 128, cs], ld)
                ldv = ld.n
                for i in range(DT):
                    p.op("scalar",
                         lambda e, i=i: e.activation(x2[i].ap(), xt[i].ap(), AF.Square),
                         waits=[(ld, ldv)] if i == 0 else [],
                         incs=[(se, 1)])
                    p.op("tensor",
                         lambda e, i=i: e.matmul(ps_s.ap()[0:1, 0:TCL], self.onesK.ap(),
                                                 xt[i].ap(), start=(i == 0), stop=(i == DT - 1)),
                         waits=[(ld, ldv)] if i == 0 else [])
                    p.op("tensor",
                         lambda e, i=i: e.matmul(ps_s.ap()[0:1, 512:512 + TCL], self.onesK.ap(),
                                                 x2[i].ap(), start=(i == 0), stop=(i == DT - 1)),
                         waits=[(se, se.n)],
                         incs=[(pe, 1)] if i == DT - 1 else [])
                pev = pe.n
                R = rows.ap()
                m_, q_, var_, rstd_, mr_ = (R[0:1, i * TCL:(i + 1) * TCL] for i in range(5))
                p.op("vector", lambda e: e.tensor_scalar_mul(m_, ps_s.ap()[0:1, 0:TCL], 1.0 / DM),
                     waits=[(pe, pev)])
                p.op("vector", lambda e: e.tensor_scalar_mul(q_, ps_s.ap()[0:1, 512:512 + TCL], 1.0 / DM))
                p.op("vector", lambda e: e.tensor_tensor(var_, m_, m_, op=ALU.mult))
                p.op("vector", lambda e: e.tensor_tensor(var_, q_, var_, op=ALU.subtract))
                p.op("vector", lambda e: e.tensor_scalar_add(var_, var_, 1e-5),
                     incs=[(ve, 1)])
                p.op("scalar", lambda e: e.activation(rstd_, var_, AF.Sqrt),
                     waits=[(ve, ve.n)], incs=[(se, 1)])
                p.op("vector", lambda e: e.reciprocal(rstd_, rstd_), waits=[(se, se.n)])
                p.op("vector", lambda e: e.tensor_tensor(mr_, m_, rstd_, op=ALU.mult),
                     incs=[(ve, 1)])
                p.op("tensor", lambda e: e.matmul(ps_b.ap()[:, 0:TCL], self.ones1.ap(), rstd_,
                                                  start=True, stop=True),
                     waits=[(ve, ve.n)])
                p.op("tensor", lambda e: e.matmul(ps_b.ap()[:, 512:512 + TCL], self.ones1.ap(), mr_,
                                                  start=True, stop=True),
                     incs=[(pe, 1)])
                pev2 = pe.n
                for i in range(DT):
                    p.op("vector",
                         lambda e, i=i: e.tensor_tensor(o1[i].ap(), xt[i].ap(),
                                                        ps_b.ap()[:, 0:TCL], op=ALU.mult),
                         waits=[(pe, pev2)])
                    last = p.op("vector",
                                lambda e, i=i: e.tensor_tensor(o1[i].ap(), o1[i].ap(),
                                                               ps_b.ap()[:, 512:512 + TCL],
                                                               op=ALU.subtract),
                                incs=[(ve, 1)])
                    if gamma is not None:
                        last = p.op("vector",
                                    lambda e, i=i: e.tensor_scalar_mul(o1[i].ap(), o1[i].ap(),
                                                                       gamma.ap()[:, i:i + 1]),
                                    incs=[(ve, 1)])
                    if dst_bf is not None:
                        p.op("scalar",
                             lambda e, i=i: e.copy(xt[i].ap().bitcast(BF16)[:, 0:TCL],
                                                   o1[i].ap()),
                             waits=[(ve, ve.n)], incs=[(se, 1)])
                        p.op("sync",
                             lambda e, i=i, cs=cs: e.dma_start(
                                 out=dst_bf[i * 128:(i + 1) * 128, cs],
                                 in_=xt[i].ap().bitcast(BF16)[:, 0:TCL]),
                             waits=[(se, se.n)], incs=[(st, 16)], is_dma=True)
                    if add is not None:
                        p.op("vector",
                             lambda e, i=i: e.tensor_tensor(o1[i].ap(), o1[i].ap(), at[i].ap(),
                                                            op=ALU.add),
                             incs=[(ve, 1)])
                    if dst_f32 is not None:
                        p.op("sync",
                             lambda e, i=i, cs=cs: e.dma_start(
                                 out=dst_f32[i * 128:(i + 1) * 128, cs], in_=o1[i].ap()),
                             waits=[(ve, ve.n)], incs=[(st, 16)], is_dma=True)
                p.bar(f"{tag}_c{c}")
        return

    CACHE_GROUPS = {
        "m0": ["m0_wi", "m0_wx", "m0_wdt", "m0_wo"],
        "m1": ["m1_wi", "m1_wx", "m1_wdt", "m1_wo"],
        "m2": ["m2_wi", "m2_wx", "m2_wdt", "m2_wo"],
        "m3": ["m3_wi", "m3_wx", "m3_wdt", "m3_wo"],
        "rest": ["f_w1", "f_w2", "f_h1", "f_h2", "f_w3",
                 "a_win", "a_wx", "a_wdt", "a_wo",
                 "g_kvd", "g_qd", "g_ku", "g_qu", "g_vu", "g_rq", "g_rk", "g_ow",
                 "g_sg", "g_su", "g_sd"]
                + [f"g_{w}{e}" for e in range(7) for w in ("rg", "ru", "rd")],
    }

    def build_cache(self, maps0):
        p, d = self.p, self.d
        self.cache = {}
        for grp, names in self.CACHE_GROUPS.items():
            sem = p.sem(f"cg_{grp}")
            for name in names:
                shape = list(maps0[name].shape)
                dst = self.nc.dram_tensor("cb_" + name, shape, BF16,
                                          kind="Internal").ap()
                p.dma("gpsimd", dst, d[name], sem)
                self.cache[name] = (dst, sem, None)
        for grp, names in self.CACHE_GROUPS.items():
            sem = p.sem(f"cg_{grp}")
            for name in names:
                dst, s, _ = self.cache[name]
                self.cache[name] = (dst, s, s.n)

    def linear(self, stk, wname, xfn, NCOL, evac, tag, first_waits=(), wq="scalar",
               m_range=None, acc_group=None):
        p = self.p
        cache_ap, csem, ccount = self.cache[wname]
        M = cache_ap.shape[0] * 128
        kp = cache_ap.shape[1]
        nk = cache_ap.shape[2] // 128
        kfree = cache_ap.shape[2]
        pe, ev = self.pe, self.ev
        wl = [p.sem("wlA"), p.sem("wlB")]
        wl_base = [wl[0].n, wl[1].n]
        wsl = p.sbuf(f"{tag}_wsl", [kp, 2 * kfree], BF16, stk)
        pe_base = pe.n
        ev_base = ev.n
        mlist = list(range(M // 128)) if m_range is None else list(m_range)
        for mi, m in enumerate(mlist):
            b = mi % 2
            lw = [(csem, ccount)]
            lw.append((pe, pe_base + mi - 1) if mi >= 2 else (pe, pe_base))
            p.op(wq, lambda e, m=m, b=b: e.dma_start(
                out=wsl.ap()[:, b * kfree:(b + 1) * kfree], in_=cache_ap[m]),
                waits=lw, incs=[(wl[b], 16)], is_dma=True)
            psum_ap = self.pp[b]
            for k in range(nk):
                w = []
                if k == 0:
                    w.append((wl[b], wl_base[b] + 16 * (mi // 2 + 1)))
                    w.append((ev, ev_base + mi - 1 if mi >= 2 else ev_base))
                    w.extend(first_waits if mi == 0 else [])
                first = (k == 0) if acc_group is None else (acc_group[0] == 0 and k == 0)
                last = (k == nk - 1) if acc_group is None else \
                    (acc_group[0] == acc_group[1] - 1 and k == nk - 1)
                x_ap = xfn(k)
                p.op("tensor",
                     lambda e, k=k, b=b, first=first, last=last, psum_ap=psum_ap,
                     x_ap=x_ap: e.matmul(
                         psum_ap[:, 0:NCOL],
                         wsl.ap()[:, b * kfree + k * 128: b * kfree + (k + 1) * 128],
                         x_ap, start=first, stop=last),
                     waits=w,
                     incs=[(pe, 1)] if k == nk - 1 else [])
            if acc_group is None or acc_group[0] == acc_group[1] - 1:
                evac(m, psum_ap, pe.n)
        return

    def build(self, maps0):
        p, d = self.p, self.d
        self.declare(maps0)
        self.load_consts()
        self.pe = p.sem("pe")
        self.ev = p.sem("ev")
        self.ppt = self.p.psum("pp", [128, 1024], F32)
        self.pp = [self.ppt.ap()[:, 0:512], self.ppt.ap()[:, 512:1024]]
        self.build_cache(maps0)

        self.ln(d["x0T"], dst_bf=d["sc_xa"][0:DM, :], tag="ln0")
        self.l0_mamba()
        self.ln(d["sc_bo"][0:DM, :], dst_f32=d["sc_ra"], add=d["x0T"], tag="ln1")
        self.ln(d["sc_ra"], dst_bf=d["sc_xa"][0:DM, :], tag="ln2")
        self.l0_ffn()
        self.ln(d["sc_bo"][0:DM, :], dst_f32=d["sc_rb"], add=d["sc_ra"], tag="ln3")
        self.ln(d["sc_rb"], dst_bf=d["sc_xa"][0:DM, :], tag="ln4")
        self.l1_mixer()
        self.ln(d["sc_bo"][0:DM, :], dst_f32=d["sc_ra"], add=d["sc_rb"], tag="ln5")
        self.ln(d["sc_ra"], dst_f32=d["sc_xg"], tag="ln6")
        self.g_ln1 = p.sbuf("g_ln1s", [128, DT], F32)
        ldg = p.sem("ldg")
        p.dma("sync", self.g_ln1.ap(), d["g_ln1"], ldg)
        self.g_ln2 = p.sbuf("g_ln2s", [128, DT], F32)
        p.dma("sync", self.g_ln2.ap(), d["g_ln2"], ldg)
        p.bar("lnw")
        self.ln(d["sc_xg"], dst_bf=d["sc_xa"][0:DM, :], gamma=self.g_ln1, tag="ln7")
        self.l1_attention()
        self.add_f32(d["sc_xg"], d["sc_bo"][0:DM, :], d["sc_t1"], tag="addt1")
        self.ln(d["sc_t1"], dst_bf=d["sc_xa"][0:DM, :], dst_f32=d["sc_xz"],
                gamma=self.g_ln2, tag="ln8")
        self.l1_moe()
        self.add_f32(d["sc_t1"], d["sc_bo"][0:DM, :], d["sc_xg"], tag="addbo")
        self.ln(d["sc_xg"], dst_f32=d["out"], add=d["sc_ra"], tag="ln9")
        fin = p.sem("fin")
        p.op("sync", lambda e: e.nop(), incs=[(fin, 1)])
        p.op("sync", lambda e: e.nop(), waits=[(fin, fin.n)])

    def l0_mamba(self, nlayers=4, nchunks=None, dbg=None, maxphase=8):
        p, d = self.p, self.d
        TC = 512
        NCH = T // TC
        E = 2048
        GT = E // 128
        NS = 16
        pe, ev = self.pe, self.ev
        with contextlib.ExitStack() as stk:
            cw = p.sbuf("mcw", [128, 64], F32, stk)
            cb = p.sbuf("mcb", [128, 16], F32, stk)
            bdt = p.sbuf("mbdt", [128, 16], F32, stk)
            Dp = p.sbuf("mD", [128, 16], F32, stk)
            alog = p.sbuf("mAl", [128, 256], F32, stk)
            negA = p.sbuf("mnA", [128, 256], F32, stk)
            sel16 = p.sbuf("msel", [16, 2048], BF16, stk)
            xk = p.sbuf("mxk", [128, 8 * TC], BF16, stk)
            u_pre = p.sbuf("mupre", [128, GT * TC], BF16, stk)
            sz = p.sbuf("msz", [128, GT * TC], BF16, stk)
            u = p.sbuf("mu", [128, GT * TC], BF16, stk)
            delta = p.sbuf("mdelta", [128, GT * TC], BF16, stk)
            du = p.sbuf("mdu", [128, GT * TC], BF16, stk)
            yout = p.sbuf("myout", [128, GT * TC], BF16, stk)
            Bb = p.sbuf("mBb", [128, NS * TC], BF16, stk)
            Cb = p.sbuf("mCb", [128, NS * TC], BF16, stk)
            halo = p.sbuf("mhalo", [128, 16 * 4], BF16, stk)
            uext = p.sbuf("muext", [128, 2 * (TC + 3)], BF16, stk)
            cacc = p.sbuf("mcacc", [128, 2 * TC], F32, stk)
            xdbl = p.sbuf("mxdbl", [128, TC], BF16, stk)
            bcb = p.sbuf("mbcb", [16, TC], BF16, stk)
            bcc = p.sbuf("mbcc", [16, TC], BF16, stk)
            spt = p.sbuf("mspt", [128, 4 * TC], F32, stk)
            dAb = p.sbuf("mdA", [128, 2 * TC], F32, stk)
            dBub = p.sbuf("mdBu", [128, 2 * TC], BF16, stk)
            hb = p.sbuf("mh", [128, 2 * TC], BF16, stk)
            zb = p.sbuf("mz", [128, 2 * TC], BF16, stk)
            carry = p.sbuf("mcarry", [128, 256], F32, stk)
            ot = p.sbuf("mot", [128, 2 * TC], F32, stk)
            pm_t = p.psum("mpm", [128, 1024], F32, stk)
            pm = [pm_t.ap()[:, 0:TC], pm_t.ap()[:, 512:512 + TC]]
            ldc = p.sem("ldc")
            ldcg = p.sem("ldcg")
            mse = p.sem("xse")
            mve = p.sem("xve")
            mgp = p.sem("xgp")
            yev = p.sem("xyev")
            sto = [p.sem("stoA"), p.sem("stoB")]
            ldx = p.sem("ldx")
            p.dma("gpsimd", sel16.ap(), d["c_sel16"], ldcg)

            for l in range(nlayers):
                xin = d["sc_xa"][0:DM, :] if l % 2 == 0 else d["sc_xb"][0:DM, :]
                if l < nlayers - 1:
                    xout = d["sc_xb"][0:DM, :] if l % 2 == 0 else d["sc_xa"][0:DM, :]
                else:
                    xout = d["sc_bo"][0:DM, :]
                l_is_last = (l == nlayers - 1)
                for t_, nm in ((cw, "cw"), (cb, "cb"), (bdt, "bdt"), (Dp, "D"),
                               (alog, "Alog")):
                    p.dma("sync", t_.ap(), d[f"m{l}_{nm}"], ldc)
                ldcv = ldc.n
                p.op("scalar", lambda e: e.activation(negA.ap(), alog.ap(), AF.Exp),
                     waits=[(ldc, ldcv)], incs=[(mse, 1)])
                p.op("vector", lambda e: e.tensor_scalar_mul(negA.ap(), negA.ap(), -1.0),
                     waits=[(mse, mse.n)])
                p.op("vector", lambda e: e.memset(halo.ap(), 0.0))
                p.op("vector", lambda e: e.memset(carry.ap(), 0.0), incs=[(mve, 1)])
                p.bar(f"m{l}_init")

                for c in range(nchunks or NCH):
                    cs = slice(c * TC, (c + 1) * TC)
                    with contextlib.ExitStack() as cstk:
                        for i in range(DT):
                            p.op("sync", lambda e, i=i, cs=cs, xin=xin: e.dma_start(
                                out=xk.ap()[:, i * TC:(i + 1) * TC],
                                in_=xin[i * 128:(i + 1) * 128, cs]),
                                incs=[(ldx, 16)], is_dma=True)
                        ldxv = ldx.n
                        xf = lambda k: xk.ap()[:, k * TC:(k + 1) * TC]

                        def evac_in(m, ps, pev):
                            if m < GT:
                                p.op("scalar", lambda e, m=m, ps=ps: e.activation(
                                    u_pre.ap()[:, m * TC:(m + 1) * TC], ps[:, 0:TC],
                                    AF.Copy), waits=[(pe, pev)], incs=[(ev, 1)])
                            else:
                                mm = m - GT
                                p.op("scalar", lambda e, mm=mm, ps=ps: e.activation(
                                    sz.ap()[:, mm * TC:(mm + 1) * TC], ps[:, 0:TC],
                                    AF.Silu), waits=[(pe, pev)], incs=[(ev, 1)])
                        self.linear(cstk, f"m{l}_wi", xf, TC, evac_in, "mwi",
                                    first_waits=[(ldx, ldxv)])
                        inproj_ev = ev.n

                        for g in range(GT):
                            b = g % 2
                            ue = uext.ap()[:, b * (TC + 3):(b + 1) * (TC + 3)]
                            ca = cacc.ap()[:, b * TC:(b + 1) * TC]
                            w0 = [(ev, inproj_ev)] if g < 2 else [(mse, mse.n)]
                            p.op("vector", lambda e, ue=ue, g=g: e.tensor_copy(
                                ue[:, 0:3], halo.ap()[:, g * 4:g * 4 + 3]), waits=w0)
                            p.op("vector", lambda e, ue=ue, g=g: e.tensor_copy(
                                ue[:, 3:TC + 3], u_pre.ap()[:, g * TC:(g + 1) * TC]))
                            p.op("vector", lambda e, g=g: e.tensor_copy(
                                halo.ap()[:, g * 4:g * 4 + 3],
                                u_pre.ap()[:, (g + 1) * TC - 3:(g + 1) * TC]))
                            p.op("vector", lambda e, ue=ue, ca=ca, g=g: e.tensor_scalar(
                                ca, ue[:, 0:TC], cw.ap()[:, 4 * g:4 * g + 1],
                                cb.ap()[:, g:g + 1], ALU.mult, ALU.add))
                            for j in range(1, 4):
                                p.op("vector", lambda e, ue=ue, ca=ca, g=g, j=j:
                                     e.scalar_tensor_tensor(
                                         ca, ue[:, j:TC + j],
                                         cw.ap()[:, 4 * g + j:4 * g + j + 1], ca,
                                         ALU.mult, ALU.add))
                            mve.inc(1)
                            self.p._dirty.add(mve)
                            self.p.ops["vector"][-1][2].append((mve, 1))
                            p.op("scalar", lambda e, ca=ca, g=g: e.activation(
                                u.ap()[:, g * TC:(g + 1) * TC], ca, AF.Silu),
                                waits=[(mve, mve.n)], incs=[(mse, 1)])
                        conv_se = mse.n

                        if maxphase < 3:
                            continue
                        ufn = lambda k: u.ap()[:, k * TC:(k + 1) * TC]

                        def evac_xd(m, ps, pev):
                            p.op("scalar", lambda e, ps=ps: e.activation(
                                xdbl.ap(), ps[:, 0:TC], AF.Copy),
                                waits=[(pe, pev)], incs=[(ev, 1)])
                        self.linear(cstk, f"m{l}_wx", ufn, TC, evac_xd, "mwx",
                                    first_waits=[(mse, conv_se)])
                        xdv = ev.n
                        p.op("sync", lambda e: e.dma_start(
                            out=bcb.ap(), in_=xdbl.ap()[64:80, :]),
                            waits=[(ev, xdv)], incs=[(ldx, 16)], is_dma=True)
                        p.op("sync", lambda e: e.dma_start(
                            out=bcc.ap(), in_=xdbl.ap()[80:96, :]),
                            incs=[(ldx, 16)], is_dma=True)
                        bcv = ldx.n

                        if maxphase < 4:
                            continue
                        dtf = lambda k: xdbl.ap()[0:64, 0:TC]

                        def evac_dt(m, ps, pev):
                            b = m % 2
                            t1 = spt.ap()[:, (2 * b) * TC:(2 * b + 1) * TC]
                            t2 = spt.ap()[:, (2 * b + 1) * TC:(2 * b + 2) * TC]
                            w = [(pe, pev)]
                            if m >= 2:
                                w.append((mve, mve.n))
                            p.op("scalar", lambda e, ps=ps, t1=t1, m=m: e.activation(
                                t1, ps[:, 0:TC], AF.Abs, bias=bdt.ap()[:, m:m + 1]),
                                waits=w)
                            p.op("scalar", lambda e, t1=t1: e.activation(
                                t1, t1, AF.Exp, scale=-1.0))
                            p.op("scalar", lambda e, t1=t1, t2=t2: e.activation(
                                t2, t1, AF.Ln, bias=1.0))
                            p.op("scalar", lambda e, ps=ps, t1=t1, m=m: e.activation(
                                t1, ps[:, 0:TC], AF.Relu, bias=bdt.ap()[:, m:m + 1]),
                                incs=[(ev, 1)])
                            p.op("vector", lambda e, t1=t1, t2=t2, m=m: e.tensor_tensor(
                                delta.ap()[:, m * TC:(m + 1) * TC], t1, t2, op=ALU.add),
                                waits=[(ev, ev.n)], incs=[(mve, 1)])
                        self.linear(cstk, f"m{l}_wdt", dtf, TC, evac_dt, "mwdt")
                        if maxphase < 5:
                            continue
                        for g in range(GT):
                            p.op("vector", lambda e, g=g: e.tensor_tensor(
                                du.ap()[:, g * TC:(g + 1) * TC],
                                delta.ap()[:, g * TC:(g + 1) * TC],
                                u.ap()[:, g * TC:(g + 1) * TC], op=ALU.mult),
                                waits=[(mse, conv_se)] if g == 0 else [],
                                incs=[(mve, 1)] if g == GT - 1 else [])
                        du_ve = mve.n

                        if maxphase < 6:
                            continue
                        bev = {}
                        cev = {}
                        for n in range(NS):
                            w = [(ldx, bcv), (yev, yev.n)] if n == 0 else \
                                [(mse, bev[n - 1])]
                            p.op("tensor", lambda e, n=n: e.matmul(
                                pm[0], sel16.ap()[:, n * 128:(n + 1) * 128],
                                bcb.ap()[:, 0:TC], start=True, stop=True),
                                waits=w, incs=[(pe, 1)])
                            p.op("scalar", lambda e, n=n: e.activation(
                                Bb.ap()[:, n * TC:(n + 1) * TC], pm[0], AF.Copy),
                                waits=[(pe, pe.n)], incs=[(mse, 1)])
                            bev[n] = mse.n
                            w = [] if n == 0 else [(mse, cev[n - 1])]
                            p.op("tensor", lambda e, n=n: e.matmul(
                                pm[1], sel16.ap()[:, n * 128:(n + 1) * 128],
                                bcc.ap()[:, 0:TC], start=True, stop=True),
                                waits=w, incs=[(pe, 1)])
                            p.op("scalar", lambda e, n=n: e.activation(
                                Cb.ap()[:, n * TC:(n + 1) * TC], pm[1], AF.Copy),
                                waits=[(pe, pe.n)], incs=[(mse, 1)])
                            cev[n] = mse.n
                        bc_se = mse.n

                        if maxphase < 7:
                            continue
                        SP = 9
                        NOGP = False
                        z_done = {}
                        ymm_done = {}
                        it = 0
                        yev_base = yev.n
                        for g in range(GT):
                            for n in range(NS):
                                b = it % 2
                                dA = dAb.ap()[:, b * TC:(b + 1) * TC]
                                dBu = dBub.ap()[:, b * TC:(b + 1) * TC]
                                h_ = hb.ap()[:, b * TC:(b + 1) * TC]
                                z_ = zb.ap()[:, b * TC:(b + 1) * TC]
                                gcol = g * 16 + n
                                w = [(mve, du_ve)] if it < 2 else [z_done[it - 2]]
                                if SP >= 1:
                                    p.op("scalar", lambda e, g=g, gcol=gcol, dA=dA:
                                         e.activation(dA, delta.ap()[:, g * TC:(g + 1) * TC],
                                                      AF.Exp,
                                                      scale=negA.ap()[:, gcol:gcol + 1]),
                                         waits=w, incs=[(mse, 1)])
                                w = [(mve, du_ve), (mse, bc_se)] if it < 2 else \
                                    [z_done[it - 2]]
                                if SP >= 2:
                                    p.op("vector" if NOGP else "gpsimd",
                                         lambda e, g=g, n=n, dBu=dBu:
                                         e.tensor_tensor(dBu,
                                                         du.ap()[:, g * TC:(g + 1) * TC],
                                                         Bb.ap()[:, n * TC:(n + 1) * TC],
                                                         op=ALU.mult),
                                         waits=w, incs=[(mgp, 1)])
                                w = [(mse, mse.n), (mgp, mgp.n)]
                                if it >= 2:
                                    w.append(z_done[it - 2])
                                if SP >= 3:
                                    p.op("vector", lambda e, dA=dA, dBu=dBu, h_=h_,
                                         gcol=gcol: e.tensor_tensor_scan(
                                             out=h_, data0=dA, data1=dBu,
                                             initial=carry.ap()[:, gcol:gcol + 1],
                                             op0=ALU.mult, op1=ALU.add),
                                         waits=w, incs=[(mve, 1)])
                                if SP >= 4:
                                    p.op("vector", lambda e, h_=h_, gcol=gcol: e.tensor_copy(
                                        carry.ap()[:, gcol:gcol + 1], h_[:, TC - 1:TC]),
                                        waits=[(mve, mve.n)], incs=[(mve, 1)])
                                else:
                                    p.op("vector", lambda e, h_=h_: e.tensor_copy(
                                        h_[:, 0:4], h_[:, 0:4]), waits=w if SP < 3 else [],
                                        incs=[(mve, 1)])
                                zeng = "vector" if (it % 2 == 0 or NOGP) else "gpsimd"
                                zw = []
                                if it >= 2:
                                    zw.append(ymm_done[it - 2])
                                if zeng == "gpsimd":
                                    zw.append((mve, mve.n))
                                zsem = mve if zeng == "vector" else mgp
                                if SP >= 5:
                                    p.op(zeng, lambda e, h_=h_, z_=z_, n=n: e.tensor_tensor(
                                        z_, h_, Cb.ap()[:, n * TC:(n + 1) * TC],
                                        op=ALU.mult), waits=zw, incs=[(zsem, 1)])
                                else:
                                    p.op(zeng, lambda e, z_=z_: e.memset(z_, 0.0),
                                         waits=zw, incs=[(zsem, 1)])
                                z_done[it] = (zsem, zsem.n)
                                yw = [z_done[it]]
                                if n == 0 and g >= 2:
                                    yw.append((yev, yev_base + g - 1))
                                if SP >= 6:
                                    p.op("tensor", lambda e, z_=z_, g=g, n=n: e.matmul(
                                        pm[g % 2], self.ident.ap(), z_,
                                        start=(n == 0), stop=(n == NS - 1)),
                                        waits=yw, incs=[(pe, 1)])
                                else:
                                    p.op("tensor", lambda e, z_=z_, g=g: e.matmul(
                                        pm[g % 2], self.ident.ap(), zb.ap()[:, 0:TC],
                                        start=True, stop=True),
                                        waits=yw, incs=[(pe, 1)])
                                ymm_done[it] = (pe, pe.n)
                                it += 1
                            yw = [(pe, pe.n)]
                            p.op("vector", lambda e, g=g: e.scalar_tensor_tensor(
                                yout.ap()[:, g * TC:(g + 1) * TC],
                                u.ap()[:, g * TC:(g + 1) * TC], Dp.ap()[:, g:g + 1],
                                pm[g % 2], ALU.mult, ALU.add), waits=yw)
                            p.op("vector", lambda e, g=g: e.tensor_tensor(
                                yout.ap()[:, g * TC:(g + 1) * TC],
                                yout.ap()[:, g * TC:(g + 1) * TC],
                                sz.ap()[:, g * TC:(g + 1) * TC], op=ALU.mult),
                                incs=[(yev, 1)])
                        if maxphase < 8:
                            continue
                        yfn = lambda k: yout.ap()[:, k * TC:(k + 1) * TC]
                        sto_base = [sto[0].n, sto[1].n]

                        def evac_out(m, ps, pev):
                            b = m % 2
                            w = [(pe, pev)]
                            if m >= 2:
                                w.append((sto[b], sto_base[b] + 16 * (m // 2)))
                            if not l_is_last:
                                p.op("scalar", lambda e, m=m, ps=ps, b=b: e.activation(
                                    ot.ap().bitcast(BF16)[:, b * TC:(b + 1) * TC],
                                    ps[:, 0:TC], AF.Copy), waits=w, incs=[(ev, 1)])
                                p.op("sync", lambda e, m=m, cs=cs, b=b, xout=xout:
                                     e.dma_start(
                                         out=xout[m * 128:(m + 1) * 128, cs],
                                         in_=ot.ap().bitcast(BF16)[:, b * TC:(b + 1) * TC]),
                                     waits=[(ev, ev.n)], incs=[(sto[b], 16)], is_dma=True)
                            else:
                                p.op("scalar", lambda e, m=m, ps=ps, b=b: e.activation(
                                    ot.ap()[:, b * TC:(b + 1) * TC], ps[:, 0:TC],
                                    AF.Copy), waits=w, incs=[(ev, 1)])
                                p.op("sync", lambda e, m=m, cs=cs, b=b, xout=xout:
                                     e.dma_start(
                                         out=xout[m * 128:(m + 1) * 128, cs],
                                         in_=ot.ap()[:, b * TC:(b + 1) * TC]),
                                     waits=[(ev, ev.n)], incs=[(sto[b], 16)], is_dma=True)
                        self.linear(cstk, f"m{l}_wo", yfn, TC, evac_out, "mwo",
                                    first_waits=[(yev, yev.n)])
                        if dbg is not None and l == 0 and c == 0:
                            p.bar("dbg")
                            dst = p.sem("stoA")
                            for nm, buf in (("u_pre", u_pre), ("u", u), ("sz", sz),
                                            ("delta", delta), ("du", du),
                                            ("xdbl", xdbl), ("Bb", Bb), ("Cb", Cb),
                                            ("yout", yout), ("carry", carry)):
                                if nm in dbg:
                                    p.dma("sync", dbg[nm], buf.ap(), dst)
                            p.bar("dbg2")
                    p.bar(f"m{l}_c{c}")

    def l0_ffn(self):
        p, d = self.p, self.d
        TC = 512
        NCH = T // TC
        pe, ev = self.pe, self.ev
        with contextlib.ExitStack() as stk:
            lds = p.sem("ldx")
            sca = {}
            for nm, w in (("s1", 32), ("s2", 32), ("sh1", 32), ("sh2", 32), ("s3", 8)):
                sca[nm] = p.sbuf(f"f{nm}", [128, w], F32, stk)
                p.dma("sync", sca[nm].ap(), d[f"f_{nm}"], lds)
            xk = p.sbuf("fxk", [128, 8 * TC], BF16, stk)
            hb = p.sbuf("fhb", [128, 32 * TC], BF16, stk)
            h2 = p.sbuf("fh2", [128, 32 * TC], BF16, stk)
            tmb = p.sbuf("ftmb", [128, 2 * TC], BF16, stk)
            ot = p.sbuf("fot", [128, 2 * TC], F32, stk)
            se = p.sem("fse")
            ve = p.sem("fve")
            sto = [p.sem("stoA"), p.sem("stoB")]
            ldx = p.sem("ldx")
            p.bar("ffn_w")
            for c in range(NCH):
                cs = slice(c * TC, (c + 1) * TC)
                with contextlib.ExitStack() as cstk:
                    for i in range(DT):
                        p.op("sync", lambda e, i=i, cs=cs: e.dma_start(
                            out=xk.ap()[:, i * TC:(i + 1) * TC],
                            in_=d["sc_xa"][i * 128:(i + 1) * 128, cs]),
                            incs=[(ldx, 16)], is_dma=True)
                    ldxv = ldx.n
                    xf = lambda k: xk.ap()[:, k * TC:(k + 1) * TC]
                    hf = lambda k: hb.ap()[:, k * TC:(k + 1) * TC]
                    h2f = lambda k: h2.ap()[:, k * TC:(k + 1) * TC]

                    def evac_b1(m, ps, pev):
                        p.op("scalar", lambda e, m=m, ps=ps: e.activation(
                            hb.ap()[:, m * TC:(m + 1) * TC], ps[:, 0:TC], AF.Silu,
                            scale=sca["s1"].ap()[:, m:m + 1]),
                            waits=[(pe, pev)], incs=[(ev, 1)])
                    self.linear(cstk, "f_w1", xf, TC, evac_b1, "fw1",
                                first_waits=[(ldx, ldxv)])

                    ve_base = ve.n
                    def evac_b2(m, ps, pev):
                        w = [(pe, pev)]
                        if m >= 2:
                            w.append((ve, ve_base + m - 1))
                        p.op("scalar", lambda e, m=m, ps=ps: e.activation(
                            tmb.ap()[:, (m % 2) * TC:(m % 2 + 1) * TC], ps[:, 0:TC],
                            AF.Copy, scale=sca["s2"].ap()[:, m:m + 1]),
                            waits=w, incs=[(ev, 1)])
                        p.op("vector", lambda e, m=m: e.tensor_tensor(
                            hb.ap()[:, m * TC:(m + 1) * TC],
                            hb.ap()[:, m * TC:(m + 1) * TC],
                            tmb.ap()[:, (m % 2) * TC:(m % 2 + 1) * TC], op=ALU.mult),
                            waits=[(ev, ev.n)], incs=[(ve, 1)])
                    self.linear(cstk, "f_w2", xf, TC, evac_b2, "fw2")

                    def evac_c(m, ps, pev):
                        p.op("scalar", lambda e, m=m, ps=ps: e.activation(
                            h2.ap()[:, m * TC:(m + 1) * TC], ps[:, 0:TC], AF.Gelu,
                            scale=sca["sh1"].ap()[:, m:m + 1]),
                            waits=[(pe, pev)], incs=[(ev, 1)])
                    self.linear(cstk, "f_h1", hf, TC, evac_c, "fh1",
                                first_waits=[(ve, ve.n)])

                    def evac_d(m, ps, pev):
                        p.op("scalar", lambda e, m=m, ps=ps: e.activation(
                            hb.ap()[:, m * TC:(m + 1) * TC], ps[:, 0:TC], AF.Gelu,
                            scale=sca["sh2"].ap()[:, m:m + 1]),
                            waits=[(pe, pev)], incs=[(ev, 1)])
                    self.linear(cstk, "f_h2", h2f, TC, evac_d, "fh2")

                    sto_base = [sto[0].n, sto[1].n]
                    def evac_e(m, ps, pev):
                        b = m % 2
                        w = [(pe, pev)]
                        if m >= 2:
                            w.append((sto[b], sto_base[b] + 16 * (m // 2)))
                        p.op("scalar", lambda e, m=m, ps=ps, b=b: e.activation(
                            ot.ap()[:, b * TC:(b + 1) * TC], ps[:, 0:TC], AF.Copy,
                            scale=sca["s3"].ap()[:, m:m + 1]),
                            waits=w, incs=[(ev, 1)])
                        p.op("sync", lambda e, m=m, cs=cs, b=b: e.dma_start(
                            out=d["sc_bo"][m * 128:(m + 1) * 128, cs],
                            in_=ot.ap()[:, b * TC:(b + 1) * TC]),
                            waits=[(ev, ev.n)], incs=[(sto[b], 16)], is_dma=True)
                    self.linear(cstk, "f_w3", hf, TC, evac_e, "fw3")
                p.bar(f"ffn_c{c}")

    def l1_mixer(self):
        p, d = self.p, self.d
        TC = 512
        NCH = T // TC
        E = 2048
        GT = 16
        pe, ev = self.pe, self.ev
        with contextlib.ExitStack() as stk:
            cw = p.sbuf("acw", [128, 64], F32, stk)
            cb = p.sbuf("acb", [128, 16], F32, stk)
            bdt = p.sbuf("abdt", [128, 16], F32, stk)
            Dp = p.sbuf("aD", [128, 16], F32, stk)
            gsel = p.sbuf("agsel", [32, 2], BF16, stk)
            e2 = p.sbuf("ae2", [2, 256], BF16, stk)
            xk = p.sbuf("axk", [128, 8 * TC], BF16, stk)
            xp_pre = p.sbuf("axpre", [128, GT * TC], BF16, stk)
            szg = p.sbuf("aszg", [128, GT * TC], BF16, stk)
            xc = p.sbuf("axc", [128, GT * TC], BF16, stk)
            dti = p.sbuf("adti", [128, GT * TC], BF16, stk)
            bc = p.sbuf("abc", [128, TC], BF16, stk)
            ac = p.sbuf("aac", [128, GT * TC], F32, stk)
            am1 = p.sbuf("aam1", [128, GT * TC], BF16, stk)
            gated = p.sbuf("agated", [128, GT * TC], BF16, stk)
            yout = p.sbuf("ayout", [128, GT * TC], BF16, stk)
            Bg = p.sbuf("aBg", [128, TC], BF16, stk)
            Cg = p.sbuf("aCg", [128, TC], BF16, stk)
            bg2 = p.sbuf("abg2", [2, TC], BF16, stk)
            halo = p.sbuf("ahalo", [128, 16 * 4], BF16, stk)
            uext = p.sbuf("auext", [128, 2 * (TC + 3)], BF16, stk)
            cacc = p.sbuf("acacc", [128, 2 * TC], F32, stk)
            spt = p.sbuf("aspt", [128, 4 * TC], F32, stk)
            hb = p.sbuf("ah", [128, 2 * TC], BF16, stk)
            carry = p.sbuf("acarry", [128, 16], F32, stk)
            ot = p.sbuf("aot", [128, 2 * TC], F32, stk)
            pm_t = p.psum("apm", [128, 1024], F32, stk)
            pm = [pm_t.ap()[:, 0:TC], pm_t.ap()[:, 512:512 + TC]]
            ldc = p.sem("ldc")
            ldcg = p.sem("ldcg")
            mse = p.sem("xse")
            mve = p.sem("xve")
            yev = p.sem("xyev")
            sto = [p.sem("stoA"), p.sem("stoB")]
            ldx = p.sem("ldx")
            for t_, nm in ((cw, "a_cw"), (cb, "a_cb"), (bdt, "a_bdt"), (Dp, "a_D")):
                p.dma("sync", t_.ap(), d[nm], ldc)
            p.dma("gpsimd", gsel.ap(), d["c_gsel"], ldcg)
            p.dma("gpsimd", e2.ap(), d["c_e2"], ldcg)
            ldcgv_gates = ldcg.n
            p.op("vector", lambda e: e.memset(halo.ap(), 0.0))
            p.op("vector", lambda e: e.memset(carry.ap(), 0.0), incs=[(mve, 1)])
            p.bar("a_init")
            xin = d["sc_xa"][0:DM, :]
            xout = d["sc_bo"][0:DM, :]
            for c in range(NCH):
                cs = slice(c * TC, (c + 1) * TC)
                with contextlib.ExitStack() as cstk:
                    for i in range(DT):
                        p.op("sync", lambda e, i=i, cs=cs: e.dma_start(
                            out=xk.ap()[:, i * TC:(i + 1) * TC],
                            in_=xin[i * 128:(i + 1) * 128, cs]),
                            incs=[(ldx, 16)], is_dma=True)
                    ldxv = ldx.n
                    xf = lambda k: xk.ap()[:, k * TC:(k + 1) * TC]

                    def evac_in(m, ps, pev):
                        if m < GT:
                            p.op("scalar", lambda e, m=m, ps=ps: e.activation(
                                xp_pre.ap()[:, m * TC:(m + 1) * TC], ps[:, 0:TC],
                                AF.Copy), waits=[(pe, pev)], incs=[(ev, 1)])
                        else:
                            mm = m - GT
                            p.op("scalar", lambda e, mm=mm, ps=ps: e.activation(
                                szg.ap()[:, mm * TC:(mm + 1) * TC], ps[:, 0:TC],
                                AF.Silu), waits=[(pe, pev)], incs=[(ev, 1)])
                    self.linear(cstk, "a_win", xf, TC, evac_in, "awin",
                                first_waits=[(ldx, ldxv)])
                    inproj_ev = ev.n

                    for g in range(GT):
                        b = g % 2
                        ue = uext.ap()[:, b * (TC + 3):(b + 1) * (TC + 3)]
                        ca = cacc.ap()[:, b * TC:(b + 1) * TC]
                        w0 = [(ev, inproj_ev)] if g < 2 else [(mse, mse.n)]
                        p.op("vector", lambda e, ue=ue, g=g: e.tensor_copy(
                            ue[:, 0:3], halo.ap()[:, g * 4:g * 4 + 3]), waits=w0)
                        p.op("vector", lambda e, ue=ue, g=g: e.tensor_copy(
                            ue[:, 3:TC + 3], xp_pre.ap()[:, g * TC:(g + 1) * TC]))
                        p.op("vector", lambda e, g=g: e.tensor_copy(
                            halo.ap()[:, g * 4:g * 4 + 3],
                            xp_pre.ap()[:, (g + 1) * TC - 3:(g + 1) * TC]))
                        p.op("vector", lambda e, ue=ue, ca=ca, g=g: e.tensor_scalar(
                            ca, ue[:, 0:TC], cw.ap()[:, 4 * g:4 * g + 1],
                            cb.ap()[:, g:g + 1], ALU.mult, ALU.add))
                        for j in range(1, 4):
                            p.op("vector", lambda e, ue=ue, ca=ca, g=g, j=j:
                                 e.scalar_tensor_tensor(
                                     ca, ue[:, j:TC + j],
                                     cw.ap()[:, 4 * g + j:4 * g + j + 1], ca,
                                     ALU.mult, ALU.add))
                        mve.inc(1)
                        self.p._dirty.add(mve)
                        self.p.ops["vector"][-1][2].append((mve, 1))
                        p.op("scalar", lambda e, ca=ca, g=g: e.activation(
                            xc.ap()[:, g * TC:(g + 1) * TC], ca, AF.Silu),
                            waits=[(mve, mve.n)], incs=[(mse, 1)])
                    conv_se = mse.n

                    xcf = lambda k: xc.ap()[:, k * TC:(k + 1) * TC]

                    def evac_xs(m, ps, pev):
                        if m < 16:
                            p.op("scalar", lambda e, m=m, ps=ps: e.activation(
                                dti.ap()[:, m * TC:(m + 1) * TC], ps[:, 0:TC],
                                AF.Copy), waits=[(pe, pev)], incs=[(ev, 1)])
                        else:
                            p.op("scalar", lambda e, ps=ps: e.activation(
                                bc.ap(), ps[:, 0:TC], AF.Copy),
                                waits=[(pe, pev)], incs=[(ev, 1)])
                    self.linear(cstk, "a_wx", xcf, TC, evac_xs, "awx",
                                first_waits=[(mse, conv_se)])

                    dtf = lambda k: dti.ap()[:, k * TC:(k + 1) * TC]

                    def evac_dt(m, ps, pev):
                        b = m % 2
                        t1 = spt.ap()[:, (2 * b) * TC:(2 * b + 1) * TC]
                        t2 = spt.ap()[:, (2 * b + 1) * TC:(2 * b + 2) * TC]
                        w = [(pe, pev)]
                        if m >= 2:
                            w.append((mve, mve.n))
                        p.op("scalar", lambda e, ps=ps, t1=t1, m=m: e.activation(
                            t1, ps[:, 0:TC], AF.Abs, bias=bdt.ap()[:, m:m + 1]),
                            waits=w)
                        p.op("scalar", lambda e, t1=t1: e.activation(
                            t1, t1, AF.Exp, scale=-1.0))
                        p.op("scalar", lambda e, t1=t1, t2=t2: e.activation(
                            t2, t1, AF.Ln, bias=1.0))
                        p.op("scalar", lambda e, ps=ps, t1=t1, m=m: e.activation(
                            t1, ps[:, 0:TC], AF.Relu, bias=bdt.ap()[:, m:m + 1]),
                            incs=[(ev, 1)])
                        p.op("vector", lambda e, t1=t1, t2=t2: e.tensor_tensor(
                            t1, t1, t2, op=ALU.add), waits=[(ev, ev.n)],
                            incs=[(mve, 1)])
                        p.op("scalar", lambda e, t1=t1, t2=t2: e.activation(
                            t2, t1, AF.Sigmoid, scale=0.1), waits=[(mve, mve.n)],
                            incs=[(mse, 1)])
                        p.op("vector", lambda e, t2=t2, m=m: e.tensor_scalar(
                            ac.ap()[:, m * TC:(m + 1) * TC], t2, 0.01, 0.99,
                            ALU.max, ALU.min), waits=[(mse, mse.n)])
                        p.op("vector", lambda e, t2=t2, m=m: e.tensor_scalar(
                            am1.ap()[:, m * TC:(m + 1) * TC], t2, -1.0, 1.0,
                            ALU.mult, ALU.add), incs=[(mve, 1)])
                    self.linear(cstk, "a_wdt", dtf, TC, evac_dt, "awdt")
                    dt_ve = mve.n

                    xsv = ev.n
                    p.op("tensor", lambda e: e.matmul(
                        pm[0][0:2, 0:TC], gsel.ap(), bc.ap()[0:32, 0:TC],
                        start=True, stop=True),
                        waits=[(ev, xsv), (yev, yev.n), (ldcg, ldcgv_gates)],
                        incs=[(pe, 1)])
                    p.op("scalar", lambda e: e.activation(
                        bg2.ap(), pm[0][0:2, 0:TC], AF.Sigmoid),
                        waits=[(pe, pe.n)], incs=[(mse, 1)])
                    p.op("tensor", lambda e: e.matmul(
                        pm[0][:, 0:TC], e2.ap()[:, 0:128], bg2.ap()[:, 0:TC],
                        start=True, stop=True), waits=[(mse, mse.n)], incs=[(pe, 1)])
                    p.op("scalar", lambda e: e.activation(
                        Bg.ap(), pm[0][:, 0:TC], AF.Copy),
                        waits=[(pe, pe.n)], incs=[(mse, 1)])
                    p.op("tensor", lambda e: e.matmul(
                        pm[1][:, 0:TC], e2.ap()[:, 128:256], bg2.ap()[:, 0:TC],
                        start=True, stop=True), waits=[(mse, mse.n)], incs=[(pe, 1)])
                    p.op("scalar", lambda e: e.activation(
                        Cg.ap(), pm[1][:, 0:TC], AF.Copy),
                        waits=[(pe, pe.n)], incs=[(mse, 1)])
                    gate_se = mse.n

                    for g in range(GT):
                        p.op("gpsimd", lambda e, g=g: e.tensor_tensor(
                            am1.ap()[:, g * TC:(g + 1) * TC],
                            am1.ap()[:, g * TC:(g + 1) * TC], Bg.ap(),
                            op=ALU.mult),
                            waits=[(mve, dt_ve), (mse, gate_se)] if g == 0 else [])
                        p.op("gpsimd", lambda e, g=g: e.tensor_tensor(
                            gated.ap()[:, g * TC:(g + 1) * TC],
                            am1.ap()[:, g * TC:(g + 1) * TC],
                            xc.ap()[:, g * TC:(g + 1) * TC], op=ALU.mult),
                            incs=[(self.p.sem("xgp"), 1)])
                    gp = self.p.sem("xgp")
                    gpv = gp.n
                    for g in range(GT):
                        b = g % 2
                        h_ = hb.ap()[:, b * TC:(b + 1) * TC]
                        w = [(gp, gpv)]
                        if g >= 2:
                            w.append((yev, yev.n))
                        p.op("vector", lambda e, g=g, h_=h_: e.tensor_tensor_scan(
                            out=h_, data0=ac.ap()[:, g * TC:(g + 1) * TC],
                            data1=gated.ap()[:, g * TC:(g + 1) * TC],
                            initial=carry.ap()[:, g:g + 1],
                            op0=ALU.mult, op1=ALU.add), waits=w, incs=[(mve, 1)])
                        p.op("vector", lambda e, g=g, h_=h_: e.tensor_copy(
                            carry.ap()[:, g:g + 1], h_[:, TC - 1:TC]),
                            waits=[(mve, mve.n)])
                        p.op("vector", lambda e, g=g, h_=h_: e.tensor_tensor(
                            yout.ap()[:, g * TC:(g + 1) * TC], h_, Cg.ap(),
                            op=ALU.mult))
                        p.op("vector", lambda e, g=g: e.scalar_tensor_tensor(
                            yout.ap()[:, g * TC:(g + 1) * TC],
                            xc.ap()[:, g * TC:(g + 1) * TC], Dp.ap()[:, g:g + 1],
                            yout.ap()[:, g * TC:(g + 1) * TC], ALU.mult, ALU.add))
                        p.op("vector", lambda e, g=g: e.tensor_tensor(
                            yout.ap()[:, g * TC:(g + 1) * TC],
                            yout.ap()[:, g * TC:(g + 1) * TC],
                            szg.ap()[:, g * TC:(g + 1) * TC], op=ALU.mult),
                            incs=[(yev, 1)])

                    yfn = lambda k: yout.ap()[:, k * TC:(k + 1) * TC]
                    sto_base = [sto[0].n, sto[1].n]

                    def evac_out(m, ps, pev):
                        b = m % 2
                        w = [(pe, pev)]
                        if m >= 2:
                            w.append((sto[b], sto_base[b] + 16 * (m // 2)))
                        p.op("scalar", lambda e, m=m, ps=ps, b=b: e.activation(
                            ot.ap()[:, b * TC:(b + 1) * TC], ps[:, 0:TC],
                            AF.Copy), waits=w, incs=[(ev, 1)])
                        p.op("sync", lambda e, m=m, cs=cs, b=b: e.dma_start(
                            out=xout[m * 128:(m + 1) * 128, cs],
                            in_=ot.ap()[:, b * TC:(b + 1) * TC]),
                            waits=[(ev, ev.n)], incs=[(sto[b], 16)], is_dma=True)
                    self.linear(cstk, "a_wo", yfn, TC, evac_out, "awo",
                                first_waits=[(yev, yev.n)])
                p.bar(f"a_c{c}")

    def l1_attention(self):
        p, d = self.p, self.d
        TC = 512
        NCH = T // TC
        NQ = T // 128
        pe, ev = self.pe, self.ev
        with contextlib.ExitStack() as stk:
            kvl = p.sbuf("gkvl", [128, 2 * T], BF16, stk)
            qlt = p.sbuf("gqlt", [128, 2 * T], BF16, stk)
            self._dbg_attn = {}
            ldx = p.sem("ldx")
            gse = p.sem("gse")
            gsd = p.sem("gsd")
            gve = p.sem("gve")
            sto = [p.sem("stoA"), p.sem("stoB")]
            xk = p.sbuf("gxk", [128, 8 * TC], BF16, stk)
            for c in range(NCH):
                cs = slice(c * TC, (c + 1) * TC)
                with contextlib.ExitStack() as cstk:
                    for i in range(DT):
                        p.op("sync", lambda e, i=i, cs=cs: e.dma_start(
                            out=xk.ap()[:, i * TC:(i + 1) * TC],
                            in_=d["sc_xa"][i * 128:(i + 1) * 128, cs]),
                            incs=[(ldx, 16)], is_dma=True)
                    ldxv = ldx.n
                    xf = lambda k: xk.ap()[:, k * TC:(k + 1) * TC]

                    def evac_kv(m, ps, pev):
                        p.op("scalar", lambda e, m=m, ps=ps, cs=cs: e.activation(
                            kvl.ap()[:, m * T + cs.start:m * T + cs.stop], ps[:, 0:TC],
                            AF.Copy), waits=[(pe, pev)], incs=[(ev, 1)])
                    self.linear(cstk, "g_kvd", xf, TC, evac_kv, "gkvd",
                                first_waits=[(ldx, ldxv)])

                    def evac_ql(m, ps, pev):
                        p.op("scalar", lambda e, m=m, ps=ps, cs=cs: e.activation(
                            qlt.ap()[:, m * T + cs.start:m * T + cs.stop], ps[:, 0:TC],
                            AF.Copy), waits=[(pe, pev)], incs=[(ev, 1)])
                    self.linear(cstk, "g_qd", xf, TC, evac_ql, "gqd")
                p.bar(f"gA1_{c}")

            kvf = lambda k: None
            with (
                contextlib.ExitStack() as gstk,
            ):
                QP = p.sbuf("gQP", [128, 4 * T], BF16, gstk)
                KP = p.sbuf("gKP", [128, 4 * T], BF16, gstk)
                VT = p.sbuf("gVT", [128, (T // 128) * 520], BF16, gstk)
                self.masks = p.sbuf("gmasks", [128, 4 * 512], F32, gstk)
                ldm = p.sem("ldc")
                p.dma("sync", self.masks.ap(), d["c_masks"], ldm)
                self._dbg_attn.update(kvl=kvl, qlt=qlt)
                Sb = p.sbuf("gSb", [128, 4096], F32, gstk)
                Pb = p.sbuf("gPb", [128, 4096], BF16, gstk)
                PTs = p.sbuf("gPTs", [128, 4096], BF16, gstk)
                ya = p.sbuf("gya", [128, 4 * 128], BF16, gstk)
                stat = p.sbuf("gstat", [128, 8], F32, gstk)
                rlrow = p.sbuf("grl", [1, 128], F32, gstk)
                pvt = p.sbuf("gpvt", [64, 128], F32, gstk)
                ptp = p.psum("gptp", [128, 1024], BF16, gstk)
                pvp = p.psum("gpvp", [128, 512], F32, gstk)
                kc = p.sbuf("gkc", [128, 4 * TC], BF16, gstk)
                qc = p.sbuf("gqc", [128, 4 * TC], BF16, gstk)
                vc = p.sbuf("gvc", [128, 4 * TC], BF16, gstk)
                qrc = p.sbuf("gqrc", [128, 2 * TC], BF16, gstk)
                krc = p.sbuf("gkrc", [128, 2 * TC], BF16, gstk)
                shf = p.sbuf("gshf", [128, 2 * TC], BF16, gstk)
                rct = p.sbuf("grct", [128, TC], BF16, gstk)
                rst = p.sbuf("grst", [128, TC], BF16, gstk)
                ldr = p.sem("ldcg")
                for grp in range(2):
                    for c in range(NCH):
                        cs = slice(c * TC, (c + 1) * TC)
                        with contextlib.ExitStack() as cstk:
                            p.op("gpsimd", lambda e, cs=cs: e.dma_start(
                                out=rct.ap(), in_=d["c_cos"][:, cs]),
                                incs=[(ldr, 16)], is_dma=True)
                            p.op("gpsimd", lambda e, cs=cs: e.dma_start(
                                out=rst.ap(), in_=d["c_sgnsin"][:, cs]),
                                incs=[(ldr, 16)], is_dma=True)
                            for i in range(DT):
                                p.op("sync", lambda e, i=i, cs=cs: e.dma_start(
                                    out=xk.ap()[:, i * TC:(i + 1) * TC],
                                    in_=d["sc_xa"][i * 128:(i + 1) * 128, cs]),
                                    incs=[(ldx, 16)], is_dma=True)
                            ldxv = ldx.n
                            ldrv = ldr.n
                            xf = lambda k: xk.ap()[:, k * TC:(k + 1) * TC]
                            kvf = lambda k: kvl.ap()[:, k * T + cs.start:k * T + cs.stop]
                            qlf = lambda k: qlt.ap()[:, k * T + cs.start:k * T + cs.stop]

                            def ev_to(buf):
                                def cb(m, ps, pev, buf=buf):
                                    mm = m % 4
                                    p.op("scalar", lambda e, mm=mm, ps=ps, buf=buf:
                                         e.activation(buf.ap()[:, mm * TC:(mm + 1) * TC],
                                                      ps[:, 0:TC], AF.Copy),
                                         waits=[(pe, pev)], incs=[(ev, 1)])
                                return cb
                            mr4 = range(grp * 4, grp * 4 + 4)
                            mr2 = range(grp * 2, grp * 2 + 2)

                            def ev_to2(buf):
                                def cb(m, ps, pev, buf=buf):
                                    mm = m % 2
                                    p.op("scalar", lambda e, mm=mm, ps=ps, buf=buf:
                                         e.activation(buf.ap()[:, mm * TC:(mm + 1) * TC],
                                                      ps[:, 0:TC], AF.Copy),
                                         waits=[(pe, pev)], incs=[(ev, 1)])
                                return cb
                            self.linear(cstk, "g_ku", kvf, TC, ev_to(kc), "gku",
                                        m_range=mr4, first_waits=[(ev, ev.n)])
                            self.linear(cstk, "g_qu", qlf, TC, ev_to(qc), "gqu",
                                        m_range=mr4)
                            self.linear(cstk, "g_vu", kvf, TC, ev_to(vc), "gvu",
                                        m_range=mr4)
                            self.linear(cstk, "g_rq", qlf, TC, ev_to2(qrc), "grq",
                                        m_range=mr2)
                            self.linear(cstk, "g_rk", xf, TC, ev_to2(krc), "grk",
                                        m_range=mr2, first_waits=[(ldx, ldxv)])
                            prj_ev = ev.n
                            for t2 in range(2):
                                sl = slice(t2 * TC, (t2 + 1) * TC)
                                p.op("vector", lambda e, sl=sl: e.stream_shuffle(
                                    shf.ap()[:, sl], qrc.ap()[:, sl],
                                    mask=list(range(16, 32)) + list(range(0, 16))),
                                    waits=[(ev, prj_ev), (ldr, ldrv)])
                                p.op("vector", lambda e, sl=sl: e.tensor_tensor(
                                    qrc.ap()[:, sl], qrc.ap()[:, sl], rct.ap(),
                                    op=ALU.mult))
                                p.op("vector", lambda e, sl=sl: e.tensor_tensor(
                                    shf.ap()[:, sl], shf.ap()[:, sl], rst.ap(),
                                    op=ALU.mult))
                                p.op("vector", lambda e, sl=sl: e.tensor_tensor(
                                    qrc.ap()[:, sl], qrc.ap()[:, sl], shf.ap()[:, sl],
                                    op=ALU.add))
                                p.op("vector", lambda e, sl=sl: e.stream_shuffle(
                                    shf.ap()[:, sl], krc.ap()[:, sl],
                                    mask=list(range(16, 32)) + list(range(0, 16))))
                                p.op("vector", lambda e, sl=sl: e.tensor_tensor(
                                    krc.ap()[:, sl], krc.ap()[:, sl], rct.ap(),
                                    op=ALU.mult))
                                p.op("vector", lambda e, sl=sl: e.tensor_tensor(
                                    shf.ap()[:, sl], shf.ap()[:, sl], rst.ap(),
                                    op=ALU.mult))
                                p.op("vector", lambda e, sl=sl: e.tensor_tensor(
                                    krc.ap()[:, sl], krc.ap()[:, sl], shf.ap()[:, sl],
                                    op=ALU.add))
                            for hl in range(8):
                                j = hl // 2
                                base = (hl % 2) * 64
                                rot_t = hl // 4
                                rot_r = (hl % 4) * 32
                                qp_sl = QP.ap()[base:base + 32, j * T + cs.start:j * T + cs.stop]
                                p.op("vector", lambda e, qp_sl=qp_sl, rot_t=rot_t,
                                     rot_r=rot_r: e.tensor_scalar_mul(
                                         qp_sl, qrc.ap()[rot_r:rot_r + 32,
                                                         rot_t * TC:(rot_t + 1) * TC],
                                         0.125))
                                kp_sl = KP.ap()[base:base + 32, j * T + cs.start:j * T + cs.stop]
                                p.op("vector", lambda e, kp_sl=kp_sl, rot_t=rot_t,
                                     rot_r=rot_r: e.tensor_copy(
                                         kp_sl, krc.ap()[rot_r:rot_r + 32,
                                                         rot_t * TC:(rot_t + 1) * TC]))
                                nt = (hl * 64 + 32) // 128
                                nr = (hl * 64 + 32) % 128
                                qp_sl2 = QP.ap()[base + 32:base + 64,
                                                 j * T + cs.start:j * T + cs.stop]
                                p.op("vector", lambda e, qp_sl2=qp_sl2, nt=nt, nr=nr:
                                     e.tensor_scalar_mul(
                                         qp_sl2, qc.ap()[nr:nr + 32,
                                                         nt * TC:(nt + 1) * TC], 0.125))
                                kp_sl2 = KP.ap()[base + 32:base + 64,
                                                 j * T + cs.start:j * T + cs.stop]
                                p.op("vector", lambda e, kp_sl2=kp_sl2, nt=nt, nr=nr:
                                     e.tensor_copy(
                                         kp_sl2, kc.ap()[nr:nr + 32,
                                                         nt * TC:(nt + 1) * TC]),
                                     incs=[(gve, 1)] if hl == 7 else [])
                            for tt4 in range(TC // 128):
                                gt_tile = c * 4 + tt4
                                for ddd in range(4):
                                    wv = []
                                    if ddd == 0:
                                        wv = [(ev, prj_ev), (gve, gve.n)]
                                    p.op("tensor", lambda e, tt4=tt4, ddd=ddd:
                                         e.transpose(
                                             ptp.ap()[:, ddd * 128:(ddd + 1) * 128],
                                             vc.ap()[:, ddd * TC + tt4 * 128:
                                                     ddd * TC + tt4 * 128 + 128],
                                             self.ident.ap()),
                                         waits=wv,
                                         incs=[(pe, 1)])
                                for hh in range(8):
                                    w2 = [(pe, pe.n)] if hh == 0 else []
                                    lastcp = p.op("vector", lambda e, gt_tile=gt_tile,
                                                  hh=hh: e.tensor_copy(
                                                      VT.ap()[:, gt_tile * 520 + hh * 65:
                                                              gt_tile * 520 + hh * 65 + 64],
                                                      ptp.ap()[:, hh * 64:hh * 64 + 64]),
                                                  waits=w2,
                                                  incs=[(gve, 1)] if hh == 7 else [])
                                p.op("vector", lambda e, gt_tile=gt_tile: e.memset(
                                    VT.ap()[:, gt_tile * 520 + 64:
                                            gt_tile * 520 + 520:65], 1.0))
                        p.bar(f"gA2_{grp}_{c}")

                    PARA = 0
                    for qi in range(NQ):
                        nblk = qi // 4 + 1
                        qs = slice(qi * 128, (qi + 1) * 128)
                        for hl in range(8):
                            j = hl // 2
                            base = (hl % 2) * 64
                            sev_prev = gse.n
                            for kb in range(nblk):
                                ks = slice(kb * 512, (kb + 1) * 512)
                                w = []
                                if kb < 2:
                                    w.append((gse, gse.n))
                                    w.append((gsd, gsd.n))
                                if kb == 0 and hl == 0 and qi == 0:
                                    w.append((gve, gve.n))
                                    w.append((ldm, 16))
                                sb_ = self.pp[kb % 2]
                                p.op("tensor", lambda e, j=j, base=base, qs=qs,
                                     ks=ks, sb_=sb_: e.matmul(
                                         sb_[:, 0:512],
                                         QP.ap()[base:base + 64,
                                                 j * T + qs.start:j * T + qs.stop],
                                         KP.ap()[base:base + 64,
                                                 j * T + ks.start:j * T + ks.stop],
                                         start=True, stop=True),
                                     waits=w + ([(gse, sev_prev + kb - 1)]
                                                if kb >= 2 else []),
                                     incs=[(pe, 1)])
                                if kb == nblk - 1:
                                    p.op("vector", lambda e, kb=kb, sb_=sb_, qi=qi:
                                         e.tensor_tensor(
                                             Sb.ap()[:, kb * 512:(kb + 1) * 512],
                                             sb_[:, 0:512],
                                             self.masks.ap()[:, (qi % 4) * 512:
                                                             (qi % 4 + 1) * 512],
                                             op=ALU.add),
                                         waits=[(pe, pe.n)], incs=[(gsd, 1)])
                                else:
                                    p.op("scalar", lambda e, kb=kb, sb_=sb_:
                                         e.activation(
                                             Sb.ap()[:, kb * 512:(kb + 1) * 512],
                                             sb_[:, 0:512], AF.Copy),
                                         waits=[(pe, pe.n)], incs=[(gse, 1)])
                            p.op("vector", lambda e, nblk=nblk: e.reduce_max(
                                stat.ap()[:, 0:1], Sb.ap()[:, 0:nblk * 512],
                                axis=AX.X, negate=True),
                                waits=[(gse, gse.n), (gsd, gsd.n)], incs=[(gve, 1)])
                            p.op("scalar", lambda e, nblk=nblk: e.activation(
                                Pb.ap()[:, 0:nblk * 512], Sb.ap()[:, 0:nblk * 512],
                                AF.Exp, bias=stat.ap()[:, 0:1]),
                                waits=[(gve, gve.n)], incs=[(gse, 1)])
                            pb_ve = gve.n
                            if getattr(self, "_sp_dump", None) is not None and \
                                    grp == 0 and qi == 1 and hl == 1:
                                p.bar("spd0")
                                dst = p.sem("stoA")
                                p.dma("sync", self._sp_dump["S"], Sb.ap()[:, 0:512], dst)
                                p.dma("sync", self._sp_dump["P"], Pb.ap()[:, 0:512], dst)
                                p.dma("sync", self._sp_dump["st"], stat.ap()[:, 0:1], dst)
                                p.bar("spd1")
                            for kb in range(nblk):
                                for s4 in range(4):
                                    p.op("tensor", lambda e, kb=kb, s4=s4: e.transpose(
                                        ptp.ap()[:, s4 * 128:(s4 + 1) * 128],
                                        Pb.ap()[:, kb * 512 + s4 * 128:
                                                kb * 512 + s4 * 128 + 128],
                                        self.ident.ap()),
                                        waits=[(gse, gse.n), (gve, gve.n),
                                               (gsd, gsd.n)]
                                        if kb == 0 and s4 == 0
                                        else ([(gve, gve.n)] if s4 == 0 else []),
                                        incs=[(pe, 1)])
                                p.op("vector", lambda e, kb=kb: e.tensor_copy(
                                    PTs.ap()[:, kb * 512:(kb + 1) * 512],
                                    ptp.ap()[:, 0:512]),
                                    waits=[(pe, pe.n)], incs=[(gve, 1)])
                            pts_ve = gve.n
                            nmm = 0
                            for kb in range(nblk):
                                for s4 in range(4):
                                    gt_tile = kb * 4 + s4
                                    w = []
                                    if nmm == 0:
                                        w = [(gve, pts_ve), (gse, gse.n),
                                             (sto[0], sto[0].n)]
                                    p.op("tensor", lambda e, gt_tile=gt_tile, kb=kb,
                                         s4=s4, hl=hl, first=(nmm == 0),
                                         nblk=nblk: e.matmul(
                                             pvp.ap()[0:65, 0:128],
                                             VT.ap()[:, gt_tile * 520 + hl * 65:
                                                     gt_tile * 520 + hl * 65 + 65],
                                             PTs.ap()[:, kb * 512 + s4 * 128:
                                                      kb * 512 + (s4 + 1) * 128],
                                             start=first, stop=(kb == nblk - 1
                                                                and s4 == 3)),
                                         waits=w,
                                         incs=[(pe, 1)] if (kb == nblk - 1 and s4 == 3)
                                         else [])
                                    nmm += 1
                            dtl = hl // 2
                            rr = (hl % 2) * 64
                            p.op("vector", lambda e: e.reciprocal(
                                rlrow.ap(), pvp.ap()[64:65, 0:128]),
                                waits=[(pe, pe.n)], incs=[(gve, 1)])
                            p.op("tensor", lambda e: e.matmul(
                                ptp.ap().bitcast(F32)[0:64, 0:128], self.ones1.ap()[0:1, 0:64],
                                rlrow.ap(), start=True, stop=True),
                                waits=[(gve, gve.n)], incs=[(pe, 1)])
                            p.op("scalar", lambda e: e.activation(
                                pvt.ap(), pvp.ap()[0:64, 0:128], AF.Copy),
                                waits=[(pe, pe.n)], incs=[(gse, 1)])
                            p.op("vector", lambda e, dtl=dtl, rr=rr: e.tensor_tensor(
                                ya.ap()[rr:rr + 64, dtl * 128:(dtl + 1) * 128],
                                pvt.ap(), ptp.ap().bitcast(F32)[0:64, 0:128],
                                op=ALU.mult),
                                waits=[(pe, pe.n), (gse, gse.n)], incs=[(gsd, 1)])
                            if PARA >= 2:
                                p.bar(f"pb_{grp}_{qi}_{hl}")
                        ya_de = gsd.n
                        for dtl in range(4):
                            gd = grp * 4 + dtl
                            p.op("sync", lambda e, dtl=dtl, gd=gd, qs=qs: e.dma_start(
                                out=d["sc_qk"][gd * 128:(gd + 1) * 128, qs],
                                in_=ya.ap()[:, dtl * 128:(dtl + 1) * 128]),
                                waits=[(gsd, ya_de)], incs=[(sto[0], 16)], is_dma=True)
                        if PARA >= 1:
                            p.bar(f"pq_{grp}_{qi}")
                        p.bar(f"gB_{grp}_{qi}") if qi % 8 == 7 else None
                    p.bar(f"gB_{grp}")
                    if getattr(self, "_attn_dump", None) is not None and grp == 0:
                        dst = p.sem("stoA")
                        for nm, buf, cols in (("QP", QP, 4 * T), ("KP", KP, 4 * T),
                                              ("VT", VT, (T // 128) * 520)):
                            if nm in self._attn_dump:
                                p.dma("sync", self._attn_dump[nm], buf.ap(), dst)
                        p.bar("gdbg")

            with contextlib.ExitStack() as cstk2:
                yk = p.sbuf("gyk", [128, 8 * TC], BF16, cstk2)
                ot = p.sbuf("got", [128, 2 * TC], F32, cstk2)
                ldy = p.sem("ldx")
                for c in range(NCH):
                    cs = slice(c * TC, (c + 1) * TC)
                    with contextlib.ExitStack() as cstk:
                        for i in range(DT):
                            p.op("gpsimd", lambda e, i=i, cs=cs: e.dma_start(
                                out=yk.ap()[:, i * TC:(i + 1) * TC],
                                in_=d["sc_qk"][i * 128:(i + 1) * 128, cs]),
                                incs=[(ldy, 16)], is_dma=True)
                        ldyv = ldy.n
                        yf = lambda k: yk.ap()[:, k * TC:(k + 1) * TC]
                        sto_base = [sto[0].n, sto[1].n]

                        def evac_o(m, ps, pev):
                            b = m % 2
                            w = [(pe, pev)]
                            if m >= 2:
                                w.append((sto[1], sto_base[1] + 16 * (m // 2)))
                            p.op("scalar", lambda e, m=m, ps=ps, b=b: e.activation(
                                ot.ap()[:, b * TC:(b + 1) * TC], ps[:, 0:TC],
                                AF.Copy), waits=w, incs=[(ev, 1)])
                            p.op("sync", lambda e, m=m, cs=cs, b=b: e.dma_start(
                                out=d["sc_bo"][m * 128:(m + 1) * 128, cs],
                                in_=ot.ap()[:, b * TC:(b + 1) * TC]),
                                waits=[(ev, ev.n)], incs=[(sto[1], 16)], is_dma=True)
                        self.linear(cstk, "g_ow", yf, TC, evac_o, "gow",
                                    first_waits=[(ldy, ldyv)])
                    p.bar(f"gC_{c}")

    def l1_moe(self):
        p, d = self.p, self.d
        TC = 512
        NCH = T // TC
        pe, ev = self.pe, self.ev
        with contextlib.ExitStack() as stk:
            xk = p.sbuf("exk", [128, 8 * TC], BF16, stk)
            xkf = p.sbuf("exkf", [128, 8 * 128], F32, stk)
            rw = p.sbuf("erw", [128, 8 * 7], F32, stk)
            rbias = p.sbuf("erb", [1, 7], F32, stk)
            biasb = p.sbuf("ebb", [128, 7], F32, stk)
            probs = p.sbuf("eprobs", [128, 7], F32, stk)
            self._moe_probs_t = None
            ptmp = p.sbuf("eptmp", [128, 16], F32, stk)
            p7 = p.sbuf("ep7", [128, 7], F32, stk)
            wgt_tm = p.sbuf("ewgt", [128, 7], BF16, stk)
            wrow = [p.sbuf(f"ewr{e}", [1, T], F32, stk) for e in range(7)]
            self._moe_wrow = wrow
            self._moe_probs = probs if False else None
            wgtb = p.sbuf("ewgtb", [128, TC], BF16, stk)
            gs = p.sbuf("egs", [128, 16 * TC], BF16, stk)
            us = p.sbuf("eus", [128, 16 * TC], BF16, stk)
            acc = p.sbuf("eacc", [128, 8 * TC], F32, stk)
            pmo_t = p.psum("epmo", [128, 1024], F32, stk)
            pmo = [pmo_t.ap()[:, 0:512], pmo_t.ap()[:, 512:1024]]
            ldx = p.sem("ldx")
            ldg = p.sem("ldcg")
            ese = p.sem("xse")
            eve = p.sem("xve")
            sto = [p.sem("stoA"), p.sem("stoB")]
            p.op("sync", lambda e: e.dma_start(
                out=rw.ap().rearrange("p (k n) -> p k n", k=8),
                in_=d["g_routerT"].rearrange("(k p) n -> p k n", p=128),
                allow_slow_non_contiguous=True),
                incs=[(ldg, 16)], is_dma=True)
            ldgv_rw = ldg.n
            p.dma("sync", rbias.ap(), d["g_rbias"], ldx)
            p.op("tensor", lambda e: e.matmul(
                pmo[0][:, 0:7], self.ones1.ap(), rbias.ap(), start=True, stop=True),
                waits=[(ldx, ldx.n), (ev, ev.n)], incs=[(pe, 1)])
            p.op("scalar", lambda e: e.activation(biasb.ap(), pmo[0][:, 0:7], AF.Copy),
                 waits=[(pe, pe.n)], incs=[(ese, 1)])
            p.bar("moe_init")
            for c in range(NCH):
                cs = slice(c * TC, (c + 1) * TC)
                with contextlib.ExitStack() as cstk:
                    for i in range(DT):
                        p.op("sync", lambda e, i=i, cs=cs: e.dma_start(
                            out=xk.ap()[:, i * TC:(i + 1) * TC],
                            in_=d["sc_xa"][i * 128:(i + 1) * 128, cs]),
                            incs=[(ldx, 16)], is_dma=True)
                    ldxv = ldx.n
                    xf = lambda k: xk.ap()[:, k * TC:(k + 1) * TC]
                    for tt in range(TC // 128):
                        if tt == 0:
                            for i in range(DT):
                                p.op("sync", lambda e, i=i, cs=cs: e.dma_start(
                                    out=xkf.ap().rearrange(
                                        "p (k f) -> p k f", k=8)[:, i, :],
                                    in_=d["sc_xz"][i * 128:(i + 1) * 128,
                                                   cs.start:cs.start + 128]),
                                    incs=[(ldx, 16)], is_dma=True)
                        else:
                            for i in range(DT):
                                p.op("sync", lambda e, i=i, cs=cs, tt=tt: e.dma_start(
                                    out=xkf.ap().rearrange(
                                        "p (k f) -> p k f", k=8)[:, i, :],
                                    in_=d["sc_xz"][i * 128:(i + 1) * 128,
                                                   cs.start + tt * 128:
                                                   cs.start + (tt + 1) * 128]),
                                    waits=[(pe, pe.n)],
                                    incs=[(ldx, 16)], is_dma=True)
                        ldfv = ldx.n
                        for k in range(DT):
                            p.op("tensor", lambda e, tt=tt, k=k: e.matmul(
                                pmo[1][:, 0:7],
                                xkf.ap()[:, k * 128:(k + 1) * 128],
                                rw.ap()[:, k * 7:(k + 1) * 7],
                                start=(k == 0), stop=(k == DT - 1)),
                                waits=[(ldx, ldfv), (ldg, ldgv_rw), (eve, eve.n)]
                                if k == 0 else [],
                                incs=[(pe, 1)] if k == DT - 1 else [])
                        p.op("vector", lambda e: e.tensor_tensor(
                            probs.ap(), pmo[1][:, 0:7], biasb.ap(), op=ALU.add),
                            waits=[(pe, pe.n), (ese, ese.n)])
                        p.op("vector", lambda e: e.drain(), incs=[(eve, 1)])
                        p.op("scalar", lambda e: e.activation(
                            probs.ap(), probs.ap(), AF.Sigmoid),
                            waits=[(eve, eve.n)])
                        p.op("scalar", lambda e: e.drain(), incs=[(ese, 1)])
                        def vstep(fn, extra=()):
                            p.op("vector", fn, waits=[(eve, eve.n)] + list(extra),
                                 incs=[(eve, 1)])
                        vstep(lambda e: e.reduce_max(
                            ptmp.ap()[:, 0:1], probs.ap(), axis=AX.X),
                            extra=[(ese, ese.n)])
                        vstep(lambda e: e.tensor_scalar(
                            p7.ap(), probs.ap(), ptmp.ap()[:, 0:1], None,
                            ALU.is_equal))
                        vstep(lambda e: e.scalar_tensor_tensor(
                            p7.ap(), p7.ap(), -1e30, probs.ap(), ALU.mult, ALU.add))
                        vstep(lambda e: e.reduce_max(
                            ptmp.ap()[:, 1:2], p7.ap(), axis=AX.X))
                        vstep(lambda e: e.tensor_tensor(
                            ptmp.ap()[:, 2:3], ptmp.ap()[:, 0:1], ptmp.ap()[:, 1:2],
                            op=ALU.add))
                        vstep(lambda e: e.reciprocal(
                            ptmp.ap()[:, 3:4], ptmp.ap()[:, 2:3]))
                        vstep(lambda e: e.tensor_scalar(
                            p7.ap(), probs.ap(), ptmp.ap()[:, 1:2], None, ALU.is_ge))
                        vstep(lambda e: e.tensor_tensor(
                            p7.ap(), p7.ap(), probs.ap(), op=ALU.mult))
                        vstep(lambda e: e.tensor_scalar(
                            wgt_tm.ap(), p7.ap(), ptmp.ap()[:, 3:4], None, ALU.mult))
                        if getattr(self, "_moe_dbg", None) is not None and c == 0:
                            dd = self._moe_dbg
                            ds = p.sem("stoA")
                            p.op("sync", lambda e, tt=tt: e.dma_start(
                                out=dd["probs"][:, tt * 7:(tt + 1) * 7],
                                in_=probs.ap()), waits=[(eve, eve.n)],
                                incs=[(ds, 16)], is_dma=True)
                            p.op("sync", lambda e, tt=tt: e.dma_start(
                                out=dd["ptmp"][:, tt * 4:(tt + 1) * 4],
                                in_=ptmp.ap()[:, 0:4]), incs=[(ds, 16)], is_dma=True)
                            p.op("sync", lambda e, tt=tt: e.dma_start(
                                out=dd["p7"][:, tt * 7:(tt + 1) * 7],
                                in_=p7.ap()), incs=[(ds, 16)], is_dma=True)
                            p.op("sync", lambda e, tt=tt: e.dma_start(
                                out=dd["wgt"][:, tt * 7:(tt + 1) * 7],
                                in_=wgt_tm.ap()), incs=[(ds, 16)], is_dma=True)
                            p.bar(f"edbg{tt}")
                        for e_ in range(7):
                            p.op("tensor", lambda e, e_=e_: e.transpose(
                                pmo_t.ap().bitcast(BF16)[0:1, 1536:1664],
                                wgt_tm.ap()[:, e_:e_ + 1], self.ident.ap()),
                                waits=[(eve, eve.n)] if e_ == 0 else
                                [(eve, eve.n)],
                                incs=[(pe, 1)])
                            p.op("vector", lambda e, e_=e_, tt=tt, cs=cs: e.tensor_copy(
                                wrow[e_].ap()[0:1, cs.start + tt * 128:
                                              cs.start + (tt + 1) * 128],
                                pmo_t.ap().bitcast(BF16)[0:1, 1536:1664]),
                                waits=[(pe, pe.n)], incs=[(eve, 1)])
                    for ei in range(8):
                      with contextlib.ExitStack() as estk:
                        if ei == 0:
                            gname, uname, dname = "g_sg", "g_su", "g_sd"
                        else:
                            gname, uname, dname = (f"g_rg{ei-1}", f"g_ru{ei-1}",
                                                   f"g_rd{ei-1}")

                        def evac_g(m, ps, pev):
                            p.op("scalar", lambda e, m=m, ps=ps: e.activation(
                                gs.ap()[:, m * TC:(m + 1) * TC], ps[:, 0:TC],
                                AF.Silu), waits=[(pe, pev)], incs=[(ev, 1)])
                        self.linear(estk, gname, xf, TC, evac_g, "eg",
                                    first_waits=[(ldx, ldxv)] if ei == 0 else ())

                        def evac_u(m, ps, pev):
                            p.op("scalar", lambda e, m=m, ps=ps: e.activation(
                                us.ap()[:, m * TC:(m + 1) * TC], ps[:, 0:TC],
                                AF.Copy), waits=[(pe, pev)], incs=[(ev, 1)])
                        self.linear(estk, uname, xf, TC, evac_u, "eu")
                        gu_ev = ev.n
                        if ei > 0:
                            p.op("tensor", lambda e, ei=ei, cs=cs: e.matmul(
                                pmo[1][:, 0:TC], self.ones1.ap(),
                                wrow[ei - 1].ap()[0:1, cs], start=True, stop=True),
                                waits=[(eve, eve.n), (ese, ese.n)], incs=[(pe, 1)])
                            p.op("scalar", lambda e: e.activation(
                                wgtb.ap(), pmo[1][:, 0:TC], AF.Copy),
                                waits=[(pe, pe.n)], incs=[(ese, 1)])
                        hse = ese.n
                        for m in range(16):
                            p.op("vector", lambda e, m=m: e.tensor_tensor(
                                gs.ap()[:, m * TC:(m + 1) * TC],
                                gs.ap()[:, m * TC:(m + 1) * TC],
                                us.ap()[:, m * TC:(m + 1) * TC], op=ALU.mult),
                                waits=[(ev, gu_ev), (ese, hse)] if m == 0 else [])
                            if ei > 0:
                                p.op("vector", lambda e, m=m: e.tensor_tensor(
                                    gs.ap()[:, m * TC:(m + 1) * TC],
                                    gs.ap()[:, m * TC:(m + 1) * TC],
                                    wgtb.ap(), op=ALU.mult),
                                    incs=[(eve, 1)] if m == 15 else [])
                            else:
                                if m == 15:
                                    self.p.ops["vector"][-1][2].append((eve, 1))
                                    eve.inc(1)
                                    self.p._dirty.add(eve)
                        h_ve = eve.n
                        hf = lambda k: gs.ap()[:, k * TC:(k + 1) * TC]

                        def evac_d(m, ps, pev, ei=ei):
                            if ei == 0:
                                p.op("scalar", lambda e, m=m, ps=ps: e.activation(
                                    acc.ap()[:, m * TC:(m + 1) * TC], ps[:, 0:TC],
                                    AF.Copy), waits=[(pe, pev)], incs=[(ev, 1)])
                            else:
                                p.op("vector", lambda e, m=m, ps=ps: e.tensor_tensor(
                                    acc.ap()[:, m * TC:(m + 1) * TC],
                                    acc.ap()[:, m * TC:(m + 1) * TC],
                                    ps[:, 0:TC], op=ALU.add),
                                    waits=[(pe, pev)], incs=[(ev, 1)])
                        self.linear(estk, dname, hf, TC, evac_d, "ed",
                                    first_waits=[(eve, h_ve)])
                    accev = ev.n
                    for m in range(DT):
                        p.op("sync", lambda e, m=m, cs=cs: e.dma_start(
                            out=d["sc_bo"][m * 128:(m + 1) * 128, cs],
                            in_=acc.ap()[:, m * TC:(m + 1) * TC]),
                            waits=[(ev, accev)], incs=[(sto[0], 16)], is_dma=True)
                p.bar(f"moe_c{c}")

    def cast_copy(self, src_bf, dst_f32, tag):
        p = self.p
        with contextlib.ExitStack() as stk:
            tb = p.sbuf(f"{tag}_b", [128, 2048], BF16, stk)
            tf = p.sbuf(f"{tag}_f", [128, 2048], F32, stk)
            ld = p.sem("ldx")
            ve = p.sem("xve")
            st = p.sem("stoA")
            for i in range(DT):
                for c in range(2):
                    cs = slice(c * 2048, (c + 1) * 2048)
                    p.dma("sync", tb.ap(), src_bf[i * 128:(i + 1) * 128, cs], ld)
                    p.op("vector", lambda e: e.tensor_copy(tf.ap(), tb.ap()),
                         waits=[(ld, ld.n)], incs=[(ve, 1)])
                    p.op("sync", lambda e, i=i, cs=cs: e.dma_start(
                        out=dst_f32[i * 128:(i + 1) * 128, cs], in_=tf.ap()),
                        waits=[(ve, ve.n)], incs=[(st, 16)], is_dma=True)
                    p.bar(f"{tag}_{i}_{c}")

    def add_f32(self, a, b, dst, tag):
        p = self.p
        with contextlib.ExitStack() as stk:
            ta = p.sbuf(f"{tag}_a", [128, 2048], F32, stk)
            tb = p.sbuf(f"{tag}_b", [128, 2048], F32, stk)
            tc = p.sbuf(f"{tag}_c", [128, 2048], F32, stk)
            ld = p.sem("ldx")
            ve = p.sem("xve")
            st = p.sem("stoA")
            for i in range(DT):
                for c in range(2):
                    cs = slice(c * 2048, (c + 1) * 2048)
                    p.dma("sync", ta.ap(), a[i * 128:(i + 1) * 128, cs], ld)
                    p.dma("sync", tb.ap(), b[i * 128:(i + 1) * 128, cs], ld)
                    p.op("vector", lambda e: e.tensor_tensor(tc.ap(), ta.ap(), tb.ap(), op=ALU.add),
                         waits=[(ld, ld.n)], incs=[(ve, 1)])
                    p.op("sync", lambda e, i=i, cs=cs: e.dma_start(
                        out=dst[i * 128:(i + 1) * 128, cs], in_=tc.ap()),
                        waits=[(ve, ve.n)], incs=[(st, 16)], is_dma=True)
                    p.bar(f"{tag}_{i}_{c}")



_CACHE = {}


def build_program(maps0):
    kb = KB()
    with kb.p.stack:
        kb.build(maps0)
        kb.p.finalize()
    return kb.p.nc


def kernel(**inputs):
    maps = shard_inputs(inputs)
    nc = build_program(maps[0])
    res = run_bass_kernel_spmd(nc, maps, core_ids=list(range(8)))
    outs = [r["out"] for r in res.results]
    full = np.stack([o.T for o in outs], 0)
    return np.ascontiguousarray(full.astype(np.float32))
